# version 02 (final)
# baseline (speedup 1.0000x reference)
import math
from contextlib import ExitStack

import numpy as np

import concourse.bass as bass
import concourse.mybir as mybir
from concourse.bass_utils import run_bass_kernel_spmd

F32 = mybir.dt.float32
BF16 = mybir.dt.bfloat16
AF = mybir.ActivationFunctionType
ALU = mybir.AluOpType
AX = mybir.AxisListType

NCORE = 8
S = 2048
D = 2048
TT = S // 128
DT = D // 128

ENGINES = ("tensor", "vector", "scalar", "gpsimd", "sync")


class P:

    def __init__(self, nc):
        self.nc = nc
        self.es = ExitStack()
        self.sems = {e: self.es.enter_context(nc.semaphore(f"sem_{e}")) for e in ENGINES}
        self.cnt = {e: 0 for e in ENGINES}
        self.seen = {e: {f: 0 for f in ENGINES} for e in ENGINES}
        self.code = {e: [] for e in ENGINES}
        self.ident = None
        self.consts = {}
        self._cpool = None
        self._cpool_n = 0

    def sbuf(self, name, shape, dtype):
        return self.es.enter_context(self.nc.sbuf_tensor(name, shape, dtype))

    def psum(self, name, shape, dtype=F32):
        return self.es.enter_context(self.nc.psum_tensor(name, shape, dtype))

    def op(self, eng, fn, inc=1, own_wait=True):
        waits = []
        for f in ENGINES:
            if f == eng:
                continue
            if self.cnt[f] > self.seen[eng][f]:
                waits.append((self.sems[f], self.cnt[f]))
                self.seen[eng][f] = self.cnt[f]
        if own_wait and eng in ("vector", "scalar", "gpsimd") and self.cnt[eng] > self.seen[eng][eng]:
            waits.append((self.sems[eng], self.cnt[eng]))
            self.seen[eng][eng] = self.cnt[eng]
        sem = self.sems[eng]
        self.cnt[eng] += inc

        def thunk(e, waits=waits, fn=fn, sem=sem, inc=inc):
            for s, v in waits:
                e.wait_ge(s, v)
            inst = fn(e)
            inst.then_inc(sem, inc)

        self.code[eng].append(thunk)

    def cf(self, value, dtype=F32):
        key = (dtype, float(value))
        if key not in self.consts:
            if self._cpool is None:
                self._cpool = self.sbuf("cpool", [128, 64], F32)
            i = self._cpool_n
            self._cpool_n += 1
            col = self._cpool[:, i:i + 1]
            self.op("vector", lambda e, col=col, value=value: e.memset(col, float(value)))
            self.nc.const_aps.register(dtype, float(value), col)
            self.consts[key] = col
        return self.consts[key]

    def dma(self, out_ap, in_ap, queue="sync", **kw):
        self.op(queue, lambda e: e.dma_start(out=out_ap, in_=in_ap, **kw), inc=16)

    def mm(self, out, lhsT, rhs, start=True, stop=True):
        self.op("tensor", lambda e: e.matmul(out, lhsT, rhs, start=start, stop=stop))

    def trans(self, out_psum, in_sbuf, ident=None):
        idn = ident if ident is not None else self.ident
        k = in_sbuf.shape[0]
        self.op("tensor", lambda e: e.transpose(out_psum, in_sbuf, idn[0:k, 0:k]))

    def v(self, fn):
        self.op("vector", fn)

    def s(self, fn):
        self.op("scalar", fn)

    def g(self, fn):
        self.op("gpsimd", fn)

    def act(self, out, in_, func, bias=0.0, scale=1.0):
        if isinstance(bias, float) and bias != 0.0:
            self.cf(bias)
        if isinstance(scale, float) and scale != 1.0:
            self.cf(scale)
        self.op("scalar", lambda e: e.activation(out, in_, func, bias=bias, scale=scale))

    def copy(self, out, in_, eng="vector"):
        if eng == "scalar":
            self.op(eng, lambda e: e.copy(out, in_))
        else:
            self.op(eng, lambda e: e.tensor_copy(out, in_))

    def finalize(self):
        nc = self.nc
        final = {e: self.cnt[e] for e in ENGINES}

        def tail(e, eng):
            for f in ENGINES:
                if f != eng and final[f] > self.seen[eng][f]:
                    e.wait_ge(self.sems[f], final[f])

        with nc.Block() as blk:
            for eng in ENGINES:
                thunks = self.code[eng]

                def body(e, thunks=thunks, eng=eng):
                    for t in thunks:
                        t(e)
                    tail(e, eng)

                getattr(blk, eng)(body)
        self.es.close()



def emit_ln_xnT(p, x_hbm, xnT, tmp_pool, eps=1e-5, out_dtype=F32):
    nc = p.nc
    xt = tmp_pool["xt"]
    sq = tmp_pool["sq"]
    st = tmp_pool["st"]
    xn = tmp_pool["xn"]
    ps = tmp_pool["ps_tr"]
    for ti in range(TT):
        p.dma(xt[:, :], x_hbm[ti * 128:(ti + 1) * 128, :])
        p.v(lambda e, xt=xt, st=st: e.reduce_sum(st[:, 0:1], xt[:, :], axis=AX.X))
        p.v(lambda e, st=st: e.tensor_scalar_mul(st[:, 1:2], st[:, 0:1], 1.0 / D))
        p.v(lambda e, xt=xt, st=st: e.tensor_scalar_sub(xt[:, :], xt[:, :], st[:, 1:2]))
        p.act(sq[:, :], xt[:, :], AF.Square)
        p.v(lambda e, sq=sq, st=st: e.reduce_sum(st[:, 2:3], sq[:, :], axis=AX.X))
        p.act(st[:, 3:4], st[:, 2:3], AF.Sqrt, bias=eps, scale=1.0 / D)
        p.v(lambda e, st=st: e.reciprocal(st[:, 4:5], st[:, 3:4]))
        p.v(lambda e, xt=xt, xn=xn, st=st: e.tensor_scalar_mul(xn[:, :], xt[:, :], st[:, 4:5]))
        for di in range(DT):
            p.trans(ps[:, :], xn[:, di * 128:(di + 1) * 128])
            p.copy(xnT[:, di * S + ti * 128: di * S + (ti + 1) * 128], ps[:, :], eng="scalar")


def xnT_col(xnT, di, t0, t1):
    return xnT[:, di * S + t0: di * S + t1]


def emit_linT(p, outT, wT_hbm, xnT, n_out, wbuf, psums, bias_hbm=None, bias_buf=None,
              src_cols=S, src_tiles=DT):
    ot_tiles = n_out // 128
    nchunk = src_cols // 512
    for oi in range(ot_tiles):
        for di in range(src_tiles):
            p.dma(wbuf[:, :], wT_hbm[di * 128:(di + 1) * 128, oi * 128:(oi + 1) * 128])
            for c in range(nchunk):
                p.mm(psums[c][:, :], wbuf[:, :], xnT_col(xnT, di, c * 512, (c + 1) * 512),
                     start=(di == 0), stop=(di == src_tiles - 1))
        if bias_hbm is not None:
            p.dma(bias_buf[:, 0:1], bias_hbm[oi * 128:(oi + 1) * 128, 0:1])
        for c in range(nchunk):
            dst = outT[:, oi * src_cols + c * 512: oi * src_cols + (c + 1) * 512]
            if bias_hbm is not None:
                p.v(lambda e, dst=dst, src=psums[c], bb=bias_buf: e.tensor_scalar_add(dst, src[:, :], bb[:, 0:1]))
            else:
                p.copy(dst, psums[c][:, :], eng="scalar")


class Arena:

    def __init__(self, p, name, cols):
        self.t = p.sbuf(name, [128, cols], F32)
        self.cols = cols
        self.off = 0

    def alloc(self, n):
        assert self.off + n <= self.cols, f"arena overflow {self.off}+{n}>{self.cols}"
        v = self.t[:, self.off:self.off + n]
        self.off += n
        return v

    def mark(self):
        return self.off

    def reset(self, to=0):
        self.off = to


def start_prog(nc, want_masks=False):
    p = P(nc)
    C = {}
    ident_h = nc.dram_tensor("c_ident", [128, 128], F32, kind="ExternalInput").ap()
    p.ident = p.sbuf("identS", [128, 128], F32)
    p.dma(p.ident[:, :], ident_h)
    C["ps4"] = [p.psum(f"ps4_{i}", [128, 512], F32) for i in range(4)]
    C["psE"] = p.psum("psE", [128, 512], F32)
    C["psT"] = p.psum("psT", [128, 128], F32)
    C["psAV"] = p.psum("psAV", [128, 136], F32)
    if want_masks:
        mh = nc.dram_tensor("c_masks", [128, 4 * 512], F32, kind="ExternalInput").ap()
        C["masks"] = p.sbuf("masksS", [128, 4 * 512], F32)
        p.dma(C["masks"][:, :], mh)
    return p, C


def emit_ln_chunk(p, C, x_hbm, xnT_c, tmp, t0_tile, ntiles, chunk_cols):
    xt, sq, st, xn = tmp["xt"], tmp["sq"], tmp["st"], tmp["xn"]
    for j in range(ntiles):
        ti = t0_tile + j
        p.dma(xt[:, :], x_hbm[ti * 128:(ti + 1) * 128, :])
        p.v(lambda e, xt=xt, st=st: e.reduce_sum(st[:, 0:1], xt[:, :], axis=AX.X))
        p.v(lambda e, st=st: e.tensor_scalar_mul(st[:, 1:2], st[:, 0:1], 1.0 / D))
        p.v(lambda e, xt=xt, st=st: e.tensor_scalar_sub(xt[:, :], xt[:, :], st[:, 1:2]))
        p.act(sq[:, :], xt[:, :], AF.Square)
        p.v(lambda e, sq=sq, st=st: e.reduce_sum(st[:, 2:3], sq[:, :], axis=AX.X))
        p.act(st[:, 3:4], st[:, 2:3], AF.Sqrt, bias=1e-5, scale=1.0 / D)
        p.v(lambda e, st=st: e.reciprocal(st[:, 4:5], st[:, 3:4]))
        p.v(lambda e, xt=xt, xn=xn, st=st: e.tensor_scalar_mul(xn[:, :], xt[:, :], st[:, 4:5]))
        for di in range(DT):
            p.trans(C["psT"][:, :], xn[:, di * 128:(di + 1) * 128])
            p.copy(xnT_c[:, di * chunk_cols + j * 128: di * chunk_cols + (j + 1) * 128],
                   C["psT"][:, :], eng="scalar")


def mk_ln_tmp(p, ar):
    return dict(xt=ar.alloc(D), sq=ar.alloc(D), st=ar.alloc(8), xn=ar.alloc(D))



def build_s5(nc):
    x = nc.dram_tensor("x", [S, D], F32, kind="ExternalInput").ap()
    wqkvT = nc.dram_tensor("wqkvT", [D, 3 * D], F32, kind="ExternalInput").ap()
    wprojT = nc.dram_tensor("wprojT", [D, D], F32, kind="ExternalInput").ap()
    bproj_bc = nc.dram_tensor("bproj_bc", [128, D], F32, kind="ExternalInput").ap()
    out = nc.dram_tensor("out", [S, D], F32, kind="ExternalOutput").ap()
    qT_d = nc.dram_tensor("qT_d", [D, S], F32, kind="Internal").ap()
    kT_d = nc.dram_tensor("kT_d", [D, S], F32, kind="Internal").ap()
    va_d = nc.dram_tensor("va_d", [S, 16 * 132], F32, kind="Internal").ap()
    aT_d = nc.dram_tensor("aT_d", [D, S], F32, kind="Internal").ap()

    p, C = start_prog(nc)
    ar = Arena(p, "bigA", 33100)
    arB = Arena(p, "bigB", 8192 + 600)
    ps4, psT, psAV = C["ps4"], C["psT"], C["psAV"]

    xnT = ar.alloc(DT * S)
    tmp = mk_ln_tmp(p, arB)
    emit_ln_chunk(p, C, x, xnT, tmp, 0, TT, S)
    arB.reset()

    wb = arB.alloc(16 * 128)
    ot = arB.alloc(S)
    va = arB.alloc(16 * 132)
    for h in range(16):
        qc = h * 384
        for (dst_d, c0) in ((qT_d, qc), (kT_d, qc + 128)):
            for di in range(DT):
                p.dma(wb[:, di * 128:(di + 1) * 128],
                      wqkvT[di * 128:(di + 1) * 128, c0:c0 + 128])
            for c in range(4):
                for di in range(DT):
                    p.mm(ps4[c][:, :], wb[:, di * 128:(di + 1) * 128],
                         xnT[:, di * S + c * 512:di * S + (c + 1) * 512],
                         start=(di == 0), stop=(di == DT - 1))
                p.copy(ot[:, c * 512:(c + 1) * 512], ps4[c][:, :], eng="scalar")
            p.dma(dst_d[h * 128:(h + 1) * 128, :], ot[:, :], queue="gpsimd")
        c0 = qc + 256
        for di in range(DT):
            p.dma(wb[:, di * 128:(di + 1) * 128],
                  wqkvT[di * 128:(di + 1) * 128, c0:c0 + 128])
        p.v(lambda e, va=va: e.memset(va[:, :], 1.0))
        for ti in range(TT):
            for di in range(DT):
                p.mm(psT[:, :], xnT[:, di * S + ti * 128:di * S + (ti + 1) * 128],
                     wb[:, di * 128:(di + 1) * 128],
                     start=(di == 0), stop=(di == DT - 1))
            p.copy(va[:, ti * 132:ti * 132 + 128], psT[:, :], eng="scalar")
        p.dma(va_d[:, h * 132:(h + 1) * 132].rearrange("(tt pp) c -> pp tt c", pp=128),
              va[:, :].rearrange("p (tt c) -> p tt c", c=132), queue="gpsimd")
    ar.reset()
    arB.reset()

    norm = 1.0 / math.sqrt(128.0)
    qh = ar.alloc(S)
    kh = ar.alloc(S)
    vah = ar.alloc(16 * 132)
    expT = ar.alloc(16 * 512)
    atT = ar.alloc(S)
    aun = arB.alloc(128)
    dg = arB.alloc(128)
    rec = arB.alloc(8)
    for h in range(16):
        p.dma(qh[:, :], qT_d[h * 128:(h + 1) * 128, :])
        p.dma(kh[:, :], kT_d[h * 128:(h + 1) * 128, :])
        p.dma(vah[:, :].rearrange("p (t c) -> p t c", c=132),
              va_d[:, h * 132:(h + 1) * 132].rearrange("(t p) c -> p t c", p=128))
        for c in range(4):
            for tk in range(TT):
                p.mm(C["psE"][:, :], kh[:, tk * 128:(tk + 1) * 128], qh[:, c * 512:(c + 1) * 512])
                p.act(expT[:, tk * 512:(tk + 1) * 512], C["psE"][:, :], AF.Exp, scale=norm)
            for j in range(4):
                tq = c * 4 + j
                for tk in range(TT):
                    p.mm(psAV[:, 0:129], expT[:, tk * 512 + j * 128:tk * 512 + (j + 1) * 128],
                         vah[:, tk * 132:tk * 132 + 129],
                         start=(tk == 0), stop=(tk == TT - 1))
                p.v(lambda e, rec=rec: e.reciprocal(rec[:, 0:1], psAV[:, 128:129]))
                p.copy(aun[:, :], psAV[:, 0:128], eng="scalar")
                p.v(lambda e, dg=dg, rec=rec: e.tensor_scalar_mul(dg[:, :], p.ident[:, :], rec[:, 0:1]))
                p.mm(psT[:, :], aun[:, :], dg[:, :])
                p.copy(atT[:, tq * 128:(tq + 1) * 128], psT[:, :], eng="scalar")
        p.dma(aT_d[h * 128:(h + 1) * 128, :], atT[:, :], queue="gpsimd")
    ar.reset()
    arB.reset()

    wpc = ar.alloc(16 * 512)
    ab = ar.alloc(128)
    bpb = ar.alloc(D)
    xr = ar.alloc(512)
    ob = ar.alloc(512)
    p.dma(bpb[:, :], bproj_bc)
    for c in range(4):
        for ei in range(DT):
            p.dma(wpc[:, ei * 512:(ei + 1) * 512],
                  wprojT[ei * 128:(ei + 1) * 128, c * 512:(c + 1) * 512])
        for ti in range(TT):
            for ei in range(DT):
                p.dma(ab[:, :], aT_d[ei * 128:(ei + 1) * 128, ti * 128:(ti + 1) * 128])
                p.mm(ps4[0][:, :], ab[:, :], wpc[:, ei * 512:(ei + 1) * 512],
                     start=(ei == 0), stop=(ei == DT - 1))
            p.dma(xr[:, :], x[ti * 128:(ti + 1) * 128, c * 512:(c + 1) * 512])
            p.v(lambda e, ob=ob, bpb=bpb, c=c: e.tensor_add(ob[:, :], ps4[0][:, :], bpb[:, c * 512:(c + 1) * 512]))
            p.v(lambda e, ob=ob, xr=xr: e.tensor_add(ob[:, :], ob[:, :], xr[:, :]))
            p.dma(out[ti * 128:(ti + 1) * 128, c * 512:(c + 1) * 512], ob[:, :], queue="gpsimd")
    p.finalize()
    return nc



NI = 5632
IT = NI // 128


def build_s2(nc):
    x = nc.dram_tensor("x", [S, D], F32, kind="ExternalInput").ap()
    gwT = nc.dram_tensor("gwT", [D, NI], F32, kind="ExternalInput").ap()
    uwT = nc.dram_tensor("uwT", [D, NI], F32, kind="ExternalInput").ap()
    dwT = nc.dram_tensor("dwT", [NI, D], F32, kind="ExternalInput").ap()
    gs = nc.dram_tensor("gs", [NI, 1], F32, kind="ExternalInput").ap()
    us = nc.dram_tensor("us", [NI, 1], F32, kind="ExternalInput").ap()
    ds_bc = nc.dram_tensor("ds_bc", [128, D], F32, kind="ExternalInput").ap()
    out = nc.dram_tensor("out", [S, D], F32, kind="ExternalOutput").ap()

    p, C = start_prog(nc)
    ar = Arena(p, "bigA", 8192 + IT * 512 + 4096)
    arB = Arena(p, "bigB", 9100)
    ps4, psE = C["ps4"], C["psE"]

    xnT_c = ar.alloc(DT * 512)
    hT = ar.alloc(IT * 512)
    dsb = ar.alloc(D)
    tmp = mk_ln_tmp(p, arB)
    wb = arB.alloc(256)
    scol = arB.alloc(8)
    t1 = arB.alloc(512)
    t2 = arB.alloc(512)
    dwb = arB.alloc(512)
    xr = arB.alloc(512)
    p.dma(dsb[:, :], ds_bc)
    for ch in range(4):
        emit_ln_chunk(p, C, x, xnT_c, tmp, ch * 4, 4, 512)
        for it in range(IT):
            for di in range(DT):
                p.dma(wb[:, 0:128], gwT[di * 128:(di + 1) * 128, it * 128:(it + 1) * 128])
                p.dma(wb[:, 128:256], uwT[di * 128:(di + 1) * 128, it * 128:(it + 1) * 128])
                p.mm(ps4[0][:, :], wb[:, 0:128], xnT_c[:, di * 512:(di + 1) * 512],
                     start=(di == 0), stop=(di == DT - 1))
                p.mm(ps4[1][:, :], wb[:, 128:256], xnT_c[:, di * 512:(di + 1) * 512],
                     start=(di == 0), stop=(di == DT - 1))
            p.dma(scol[:, 0:1], gs[it * 128:(it + 1) * 128, :])
            p.dma(scol[:, 1:2], us[it * 128:(it + 1) * 128, :])
            p.act(t1[:, :], ps4[0][:, :], AF.Silu, scale=scol[:, 0:1])
            p.v(lambda e, t2=t2, scol=scol: e.tensor_scalar_mul(t2[:, :], ps4[1][:, :], scol[:, 1:2]))
            p.v(lambda e, t1=t1, t2=t2, it=it: e.tensor_mul(hT[:, it * 512:(it + 1) * 512], t1[:, :], t2[:, :]))
        for c2 in range(4):
            for it in range(IT):
                p.dma(dwb[:, :], dwT[it * 128:(it + 1) * 128, c2 * 512:(c2 + 1) * 512])
                for j in range(4):
                    p.mm(ps4[j][:, :], hT[:, it * 512 + j * 128:it * 512 + (j + 1) * 128],
                         dwb[:, :], start=(it == 0), stop=(it == IT - 1))
            for j in range(4):
                ti = ch * 4 + j
                p.dma(xr[:, :], x[ti * 128:(ti + 1) * 128, c2 * 512:(c2 + 1) * 512])
                p.v(lambda e, j=j, c2=c2: e.tensor_mul(t1[:, :], ps4[j][:, :], dsb[:, c2 * 512:(c2 + 1) * 512]))
                p.v(lambda e, xr=xr: e.tensor_add(t2[:, :], t1[:, :], xr[:, :]))
                p.dma(out[ti * 128:(ti + 1) * 128, c2 * 512:(c2 + 1) * 512], t2[:, :], queue="gpsimd")
    p.finalize()
    return nc



NH_E = 1408
ITE = NH_E // 128


def build_s6(nc):
    x = nc.dram_tensor("x", [S, D], F32, kind="ExternalInput").ap()
    wgT = nc.dram_tensor("wgT", [D, 8], F32, kind="ExternalInput").ap()
    w1T = nc.dram_tensor("w1T", [8 * D, NH_E], F32, kind="ExternalInput").ap()
    w2T = nc.dram_tensor("w2T", [8 * D, NH_E], F32, kind="ExternalInput").ap()
    w3T = nc.dram_tensor("w3T", [8 * NH_E, D], F32, kind="ExternalInput").ap()
    out = nc.dram_tensor("out", [S, D], F32, kind="ExternalOutput").ap()

    p, C = start_prog(nc)
    ar = Arena(p, "bigA", 8192 + 8192 + ITE * 512 + 64)
    arB = Arena(p, "bigB", 8192)
    ps4, psE = C["ps4"], C["psE"]

    xn_c = ar.alloc(DT * 512)
    y_c = ar.alloc(4 * D)
    hT = ar.alloc(ITE * 512)
    wtok = ar.alloc(32)
    tmp = mk_ln_tmp(p, arB)
    wb = arB.alloc(256)
    wgb = arB.alloc(DT * 8)
    w3b = arB.alloc(512)
    t1 = arB.alloc(512)
    rt = arB.alloc(64)
    xr = arB.alloc(512)

    for di in range(DT):
        p.dma(wgb[:, di * 8:(di + 1) * 8], wgT[di * 128:(di + 1) * 128, :])

    for ch in range(4):
        emit_ln_chunk(p, C, x, xn_c, tmp, ch * 4, 4, 512)
        for j in range(4):
            for di in range(DT):
                p.mm(psE[:, 0:8], xn_c[:, di * 512 + j * 128:di * 512 + (j + 1) * 128],
                     wgb[:, di * 8:(di + 1) * 8], start=(di == 0), stop=(di == DT - 1))
            sc = rt[:, 0:8]
            m1 = rt[:, 8:9]
            m2 = rt[:, 9:10]
            m12 = rt[:, 10:11]
            eq = rt[:, 16:24]
            sp = rt[:, 24:32]
            sg = rt[:, 32:40]
            ge2 = rt[:, 40:48]
            p.copy(sc, psE[:, 0:8], eng="scalar")
            p.v(lambda e, sc=sc, m1=m1: e.reduce_max(m1, sc, axis=AX.X))
            p.v(lambda e, sc=sc, m1=m1, eq=eq: e.tensor_single_scalar(eq, sc, m1, ALU.is_ge))
            p.v(lambda e, eq=eq, sc=sc, sp=sp: e.scalar_tensor_tensor(sp, eq, -1e30, sc, ALU.mult, ALU.add))
            p.v(lambda e, sp=sp, m2=m2: e.reduce_max(m2, sp, axis=AX.X))
            p.v(lambda e, m1=m1, m2=m2, m12=m12: e.tensor_add(m12, m1, m2))
            p.v(lambda e, sc=sc, m12=m12, sp=sp: e.tensor_scalar(sp, sc, 2.0, m12, ALU.mult, ALU.subtract))
            p.act(sg, sp, AF.Sigmoid)
            p.v(lambda e, sc=sc, m2=m2, ge2=ge2: e.tensor_single_scalar(ge2, sc, m2, ALU.is_ge))
            p.v(lambda e, sg=sg, ge2=ge2, j=j: e.tensor_mul(wtok[:, j * 8:(j + 1) * 8], sg, ge2))
        for j in range(4):
            p.v(lambda e, j=j: e.memset(y_c[:, j * D:(j + 1) * D], 0.0))
        for ex in range(8):
            for it in range(ITE):
                for di in range(DT):
                    p.dma(wb[:, 0:128], w1T[(ex * DT + di) * 128:(ex * DT + di + 1) * 128,
                                            it * 128:(it + 1) * 128])
                    p.dma(wb[:, 128:256], w2T[(ex * DT + di) * 128:(ex * DT + di + 1) * 128,
                                              it * 128:(it + 1) * 128])
                    p.mm(ps4[0][:, :], wb[:, 0:128], xn_c[:, di * 512:(di + 1) * 512],
                         start=(di == 0), stop=(di == DT - 1))
                    p.mm(ps4[1][:, :], wb[:, 128:256], xn_c[:, di * 512:(di + 1) * 512],
                         start=(di == 0), stop=(di == DT - 1))
                p.act(t1[:, :], ps4[0][:, :], AF.Silu)
                p.v(lambda e, t1=t1, it=it: e.tensor_mul(hT[:, it * 512:(it + 1) * 512], t1[:, :], ps4[1][:, :]))
            for c2 in range(4):
                for it in range(ITE):
                    p.dma(w3b[:, :], w3T[(ex * ITE + it) * 128:(ex * ITE + it + 1) * 128,
                                         c2 * 512:(c2 + 1) * 512])
                    for j in range(4):
                        p.mm(ps4[j][:, :], hT[:, it * 512 + j * 128:it * 512 + (j + 1) * 128],
                             w3b[:, :], start=(it == 0), stop=(it == ITE - 1))
                for j in range(4):
                    ys = y_c[:, j * D + c2 * 512:j * D + (c2 + 1) * 512]
                    p.v(lambda e, ys=ys, j=j, ex=ex: e.scalar_tensor_tensor(
                        ys, ps4[j][:, :], wtok[:, j * 8 + ex:j * 8 + ex + 1], ys, ALU.mult, ALU.add))
        for j in range(4):
            ti = ch * 4 + j
            for c2 in range(4):
                p.dma(xr[:, :], x[ti * 128:(ti + 1) * 128, c2 * 512:(c2 + 1) * 512])
                p.v(lambda e, xr=xr, j=j, c2=c2: e.tensor_add(
                    t1[:, :], y_c[:, j * D + c2 * 512:j * D + (c2 + 1) * 512], xr[:, :]))
                p.dma(out[ti * 128:(ti + 1) * 128, c2 * 512:(c2 + 1) * 512], t1[:, :], queue="gpsimd")
    p.finalize()
    return nc



def emit_rope2(p, qrA, qrB, dst, csS):
    p.v(lambda e: e.tensor_mul(qrA[0:64, :], qrA[0:64, :], csS[0:64, 0:S]))
    p.v(lambda e: e.tensor_mul(qrB[0:64, :], qrB[0:64, :], csS[0:64, S:2 * S]))
    p.v(lambda e: e.tensor_add(dst[0:64, :], qrA[0:64, :], qrB[0:64, :]))


def build_s1(nc):
    x = nc.dram_tensor("x", [S, D], F32, kind="ExternalInput").ap()
    wdqT = nc.dram_tensor("wdqT", [D, 512], F32, kind="ExternalInput").ap()
    wdkvT = nc.dram_tensor("wdkvT", [D, 512], F32, kind="ExternalInput").ap()
    wuqT = nc.dram_tensor("wuqT", [512, D], F32, kind="ExternalInput").ap()
    wukT = nc.dram_tensor("wukT", [512, D], F32, kind="ExternalInput").ap()
    wuvT = nc.dram_tensor("wuvT", [512, D], F32, kind="ExternalInput").ap()
    wrqTp = nc.dram_tensor("wrqTp", [512, 2048], F32, kind="ExternalInput").ap()
    wrkTp = nc.dram_tensor("wrkTp", [D, 128], F32, kind="ExternalInput").ap()
    woT = nc.dram_tensor("woT", [D, D], F32, kind="ExternalInput").ap()
    cs_c = nc.dram_tensor("cs_c", [64, 2 * S], F32, kind="ExternalInput").ap()
    out = nc.dram_tensor("out", [S, D], F32, kind="ExternalOutput").ap()
    cq_d = nc.dram_tensor("cq_d", [512, S], F32, kind="Internal").ap()
    ckv_d = nc.dram_tensor("ckv_d", [512, S], F32, kind="Internal").ap()
    aT_d = nc.dram_tensor("aT_d", [D, S], F32, kind="Internal").ap()

    p, C = start_prog(nc, want_masks=True)
    ar = Arena(p, "bigA", 32768)
    arB = Arena(p, "bigB", 6700)
    ps4, psT, psE, psAV = C["ps4"], C["psT"], C["psE"], C["psAV"]
    masks = C["masks"]

    krT = p.sbuf("krT", [128, S], F32)
    csS = p.sbuf("csS", [64, 2 * S], F32)
    p.dma(csS[:, :], cs_c)

    xnT = ar.alloc(DT * S)
    tmp = mk_ln_tmp(p, arB)
    emit_ln_chunk(p, C, x, xnT, tmp, 0, TT, S)
    arB.reset()

    wbuf = arB.alloc(128)
    ot1 = arB.alloc(S)
    for (wT_h, dst) in ((wdqT, cq_d), (wdkvT, ckv_d)):
        for oi in range(4):
            for di in range(DT):
                p.dma(wbuf[:, :], wT_h[di * 128:(di + 1) * 128, oi * 128:(oi + 1) * 128])
                for c in range(4):
                    p.mm(ps4[c][:, :], wbuf[:, :], xnT_col(xnT, di, c * 512, (c + 1) * 512),
                         start=(di == 0), stop=(di == DT - 1))
            for c in range(4):
                p.copy(ot1[:, c * 512:(c + 1) * 512], ps4[c][:, :], eng="scalar")
            p.dma(dst[oi * 128:(oi + 1) * 128, :], ot1[:, :], queue="gpsimd")
    arB.reset()
    wbuf = arB.alloc(128)
    krA = arB.alloc(S)
    krB = arB.alloc(S)
    for c in range(4):
        for di in range(DT):
            p.dma(wbuf[:, :], wrkTp[di * 128:(di + 1) * 128, :])
            p.mm(ps4[0][0:64, :], wbuf[:, 0:64], xnT_col(xnT, di, c * 512, (c + 1) * 512),
                 start=(di == 0), stop=(di == DT - 1))
            p.mm(ps4[1][0:64, :], wbuf[:, 64:128], xnT_col(xnT, di, c * 512, (c + 1) * 512),
                 start=(di == 0), stop=(di == DT - 1))
        p.copy(krA[0:64, c * 512:(c + 1) * 512], ps4[0][0:64, :], eng="scalar")
        p.copy(krB[0:64, c * 512:(c + 1) * 512], ps4[1][0:64, :], eng="scalar")
    emit_rope2(p, krA, krB, krT, csS)
    ar.reset()
    arB.reset()

    cbuf = ar.alloc(4 * S)
    qh = ar.alloc(S)
    kh = ar.alloc(S)
    qro = ar.alloc(S)
    qrp = ar.alloc(S)
    va = ar.alloc(16 * 132)
    expT = ar.alloc(16 * 512)
    atT = ar.alloc(S)
    wqb = arB.alloc(4 * 128)
    wkb = arB.alloc(4 * 128)
    wvb = arB.alloc(4 * 128)
    wrb = arB.alloc(4 * 128)
    sclip = arB.alloc(512)
    aun = arB.alloc(128)
    dg = arB.alloc(128)
    rec = arB.alloc(8)
    inv_sq = 1.0 / math.sqrt(192.0)
    for h in range(16):
        for li in range(4):
            p.dma(wqb[:, li * 128:(li + 1) * 128], wuqT[li * 128:(li + 1) * 128, h * 128:(h + 1) * 128])
            p.dma(wkb[:, li * 128:(li + 1) * 128], wukT[li * 128:(li + 1) * 128, h * 128:(h + 1) * 128])
            p.dma(wvb[:, li * 128:(li + 1) * 128], wuvT[li * 128:(li + 1) * 128, h * 128:(h + 1) * 128])
            p.dma(wrb[:, li * 128:(li + 1) * 128], wrqTp[li * 128:(li + 1) * 128, h * 128:(h + 1) * 128])
            p.dma(cbuf[:, li * S:(li + 1) * S], cq_d[li * 128:(li + 1) * 128, :])
        for c in range(4):
            for li in range(4):
                p.mm(ps4[0][:, :], wqb[:, li * 128:(li + 1) * 128],
                     cbuf[:, li * S + c * 512:li * S + (c + 1) * 512], start=(li == 0), stop=(li == 3))
                p.mm(ps4[2][0:64, :], wrb[:, li * 128:li * 128 + 64],
                     cbuf[:, li * S + c * 512:li * S + (c + 1) * 512], start=(li == 0), stop=(li == 3))
                p.mm(ps4[3][0:64, :], wrb[:, li * 128 + 64:(li + 1) * 128],
                     cbuf[:, li * S + c * 512:li * S + (c + 1) * 512], start=(li == 0), stop=(li == 3))
            p.copy(qh[:, c * 512:(c + 1) * 512], ps4[0][:, :], eng="scalar")
            p.copy(qrp[0:64, c * 512:(c + 1) * 512], ps4[2][0:64, :], eng="scalar")
            p.copy(qro[0:64, c * 512:(c + 1) * 512], ps4[3][0:64, :], eng="scalar")
        emit_rope2(p, qrp, qro, qro, csS)
        for li in range(4):
            p.dma(cbuf[:, li * S:(li + 1) * S], ckv_d[li * 128:(li + 1) * 128, :])
        for c in range(4):
            for li in range(4):
                p.mm(ps4[1][:, :], wkb[:, li * 128:(li + 1) * 128],
                     cbuf[:, li * S + c * 512:li * S + (c + 1) * 512], start=(li == 0), stop=(li == 3))
            p.copy(kh[:, c * 512:(c + 1) * 512], ps4[1][:, :], eng="scalar")
        p.v(lambda e, va=va: e.memset(va[:, :], 1.0))
        for ti in range(TT):
            for li in range(4):
                p.mm(psT[:, :], cbuf[:, li * S + ti * 128:li * S + (ti + 1) * 128],
                     wvb[:, li * 128:(li + 1) * 128], start=(li == 0), stop=(li == 3))
            p.copy(va[:, ti * 132:ti * 132 + 128], psT[:, :], eng="scalar")
        for c in range(4):
            for tk in range(4 * c + 4):
                p.mm(psE[:, :], kh[:, tk * 128:(tk + 1) * 128], qh[:, c * 512:(c + 1) * 512],
                     start=True, stop=False)
                p.mm(psE[:, :], krT[0:64, tk * 128:(tk + 1) * 128], qro[0:64, c * 512:(c + 1) * 512],
                     start=False, stop=True)
                p.v(lambda e, sclip=sclip: e.tensor_scalar(sclip[:, :], psE[:, :], 80.0, -80.0, ALU.min, ALU.max))
                dst = expT[:, tk * 512:(tk + 1) * 512]
                p.act(dst, sclip[:, :], AF.Exp, scale=inv_sq)
                r = tk - 4 * c
                if r >= 0:
                    p.v(lambda e, dst=dst, r=r: e.tensor_mul(dst, dst, masks[:, r * 512:(r + 1) * 512]))
            for j in range(4):
                tq = c * 4 + j
                for tk in range(tq + 1):
                    p.mm(psAV[:, 0:129], expT[:, tk * 512 + j * 128:tk * 512 + (j + 1) * 128],
                         va[:, tk * 132:tk * 132 + 129], start=(tk == 0), stop=(tk == tq))
                p.v(lambda e, rec=rec: e.reciprocal(rec[:, 0:1], psAV[:, 128:129]))
                p.copy(aun[:, :], psAV[:, 0:128], eng="scalar")
                p.v(lambda e, dg=dg, rec=rec: e.tensor_scalar_mul(dg[:, :], p.ident[:, :], rec[:, 0:1]))
                p.mm(psT[:, :], aun[:, :], dg[:, :])
                p.copy(atT[:, tq * 128:(tq + 1) * 128], psT[:, :], eng="scalar")
        p.dma(aT_d[h * 128:(h + 1) * 128, :], atT[:, :], queue="gpsimd")
    ar.reset()
    arB.reset()

    wpc = ar.alloc(16 * 512)
    ab = ar.alloc(128)
    xr = ar.alloc(512)
    ob = ar.alloc(512)
    for c in range(4):
        for ei in range(DT):
            p.dma(wpc[:, ei * 512:(ei + 1) * 512],
                  woT[ei * 128:(ei + 1) * 128, c * 512:(c + 1) * 512])
        for ti in range(TT):
            for ei in range(DT):
                p.dma(ab[:, :], aT_d[ei * 128:(ei + 1) * 128, ti * 128:(ti + 1) * 128])
                p.mm(ps4[0][:, :], ab[:, :], wpc[:, ei * 512:(ei + 1) * 512],
                     start=(ei == 0), stop=(ei == DT - 1))
            p.dma(xr[:, :], x[ti * 128:(ti + 1) * 128, c * 512:(c + 1) * 512])
            p.v(lambda e, ob=ob, xr=xr: e.tensor_add(ob[:, :], ps4[0][:, :], xr[:, :]))
            p.dma(out[ti * 128:(ti + 1) * 128, c * 512:(c + 1) * 512], ob[:, :], queue="gpsimd")
    p.finalize()
    return nc



def build_s3(nc):
    x = nc.dram_tensor("x", [S, D], F32, kind="ExternalInput").ap()
    wk = nc.dram_tensor("wk", [D, D], F32, kind="ExternalInput").ap()
    wq = nc.dram_tensor("wq", [D, D], F32, kind="ExternalInput").ap()
    wv = nc.dram_tensor("wv", [D, D], F32, kind="ExternalInput").ap()
    bk = nc.dram_tensor("bk", [D, 1], F32, kind="ExternalInput").ap()
    bq = nc.dram_tensor("bq", [D, 1], F32, kind="ExternalInput").ap()
    bv = nc.dram_tensor("bv", [D, 1], F32, kind="ExternalInput").ap()
    wkl = nc.dram_tensor("wkl", [D, 1024], F32, kind="ExternalInput").ap()
    wql = nc.dram_tensor("wql", [D, 1024], F32, kind="ExternalInput").ap()
    wvl = nc.dram_tensor("wvl", [D, 1024], F32, kind="ExternalInput").ap()
    bkl = nc.dram_tensor("bkl", [1024, 1], F32, kind="ExternalInput").ap()
    bql = nc.dram_tensor("bql", [1024, 1], F32, kind="ExternalInput").ap()
    bvl_bc = nc.dram_tensor("bvl_bc", [128, 1024], F32, kind="ExternalInput").ap()
    wo = nc.dram_tensor("wo", [1024, D], F32, kind="ExternalInput").ap()
    bo_bc = nc.dram_tensor("bo_bc", [128, D], F32, kind="ExternalInput").ap()
    out = nc.dram_tensor("out", [S, D], F32, kind="ExternalOutput").ap()
    k2_d = nc.dram_tensor("k2_d", [1024, S], F32, kind="Internal").ap()
    q2_d = nc.dram_tensor("q2_d", [1024, S], F32, kind="Internal").ap()
    va_d = nc.dram_tensor("va_d", [S, 16 * 66], F32, kind="Internal").ap()

    p, C = start_prog(nc, want_masks=True)
    ar = Arena(p, "bigA", 33100)
    arB = Arena(p, "bigB", 9000)
    ps4, psT, psE, psAV = C["ps4"], C["psT"], C["psE"], C["psAV"]
    masks = C["masks"]

    xn_c = ar.alloc(DT * 512)
    p1T = ar.alloc(DT * 512)
    vbuf = ar.alloc(16 * 66)
    bvlS = ar.alloc(1024)
    tmp = mk_ln_tmp(p, arB)
    wb = arB.alloc(128)
    wlb = arB.alloc(1024)
    bcol = arB.alloc(8)
    t2v = arB.alloc(1024)
    p.dma(bvlS[:, :], bvl_bc)
    for ch in range(4):
        emit_ln_chunk(p, C, x, xn_c, tmp, ch * 4, 4, 512)
        for (w1, b1, w2, b2, dst2) in ((wk, bk, wkl, bkl, k2_d), (wq, bq, wql, bql, q2_d),
                                       (wv, bv, wvl, None, None)):
            for oi in range(DT):
                for di in range(DT):
                    p.dma(wb[:, :], w1[di * 128:(di + 1) * 128, oi * 128:(oi + 1) * 128])
                    p.mm(psE[:, :], wb[:, :], xn_c[:, di * 512:(di + 1) * 512],
                         start=(di == 0), stop=(di == DT - 1))
                p.dma(bcol[:, 0:1], b1[oi * 128:(oi + 1) * 128, :])
                p.v(lambda e, oi=oi, bcol=bcol: e.tensor_scalar_add(
                    p1T[:, oi * 512:(oi + 1) * 512], psE[:, :], bcol[:, 0:1]))
            if dst2 is not None:
                for half in range(2):
                    for oi in range(DT):
                        for ll in range(4):
                            li = half * 4 + ll
                            p.dma(wlb[:, ll * 128:(ll + 1) * 128],
                                  w2[oi * 128:(oi + 1) * 128, li * 128:(li + 1) * 128])
                            p.mm(ps4[ll][:, :], wlb[:, ll * 128:(ll + 1) * 128],
                                 p1T[:, oi * 512:(oi + 1) * 512],
                                 start=(oi == 0), stop=(oi == DT - 1))
                    for ll in range(4):
                        li = half * 4 + ll
                        p.dma(bcol[:, 1:2], b2[li * 128:(li + 1) * 128, :])
                        p.v(lambda e, ll=ll, bcol=bcol: e.tensor_scalar_add(
                            t2v[:, 0:512], ps4[ll][:, :], bcol[:, 1:2]))
                        p.dma(dst2[li * 128:(li + 1) * 128, ch * 512:(ch + 1) * 512],
                              t2v[:, 0:512], queue="gpsimd")
            else:
                for j in range(4):
                    for oi in range(DT):
                        p.dma(wlb[:, 0:512], w2[oi * 128:(oi + 1) * 128, 0:512])
                        p.dma(wlb[:, 512:1024], w2[oi * 128:(oi + 1) * 128, 512:1024])
                        p.mm(ps4[0][:, :], p1T[:, oi * 512 + j * 128:oi * 512 + (j + 1) * 128],
                             wlb[:, 0:512], start=(oi == 0), stop=(oi == DT - 1))
                        p.mm(ps4[1][:, :], p1T[:, oi * 512 + j * 128:oi * 512 + (j + 1) * 128],
                             wlb[:, 512:1024], start=(oi == 0), stop=(oi == DT - 1))
                    p.v(lambda e: e.tensor_add(t2v[:, 0:512], ps4[0][:, :], bvlS[:, 0:512]))
                    p.v(lambda e: e.tensor_add(t2v[:, 512:1024], ps4[1][:, :], bvlS[:, 512:1024]))
                    p.v(lambda e: e.memset(vbuf[:, :], 1.0))
                    for h in range(16):
                        p.copy(vbuf[:, h * 66:h * 66 + 64], t2v[:, h * 64:(h + 1) * 64], eng="scalar")
                    ti = ch * 4 + j
                    p.dma(va_d[ti * 128:(ti + 1) * 128, :], vbuf[:, :], queue="gpsimd")
    ar.reset()
    arB.reset()

    attnT = ar.alloc(8 * S)
    k2h = ar.alloc(S)
    q2h = ar.alloc(S)
    vab = ar.alloc(16 * 66)
    expT = ar.alloc(16 * 512)
    aun = arB.alloc(64)
    dg = arB.alloc(128)
    rec = arB.alloc(8)
    inv_sq = 1.0 / 8.0
    for h in range(16):
        p.dma(k2h[0:64, :], k2_d[h * 64:(h + 1) * 64, :])
        p.dma(q2h[0:64, :], q2_d[h * 64:(h + 1) * 64, :])
        for tk in range(TT):
            p.dma(vab[:, tk * 66:(tk + 1) * 66], va_d[tk * 128:(tk + 1) * 128, h * 66:(h + 1) * 66])
        for c in range(4):
            for tk in range(4 * c + 4):
                p.mm(psE[:, :], k2h[0:64, tk * 128:(tk + 1) * 128], q2h[0:64, c * 512:(c + 1) * 512])
                dst = expT[:, tk * 512:(tk + 1) * 512]
                p.act(dst, psE[:, :], AF.Exp, scale=inv_sq)
                r = tk - 4 * c
                if r >= 0:
                    p.v(lambda e, dst=dst, r=r: e.tensor_mul(dst, dst, masks[:, r * 512:(r + 1) * 512]))
            for j in range(4):
                tq = c * 4 + j
                for tk in range(tq + 1):
                    p.mm(psAV[:, 0:65], expT[:, tk * 512 + j * 128:tk * 512 + (j + 1) * 128],
                         vab[:, tk * 66:tk * 66 + 65], start=(tk == 0), stop=(tk == tq))
                p.v(lambda e, rec=rec: e.reciprocal(rec[:, 0:1], psAV[:, 64:65]))
                p.copy(aun[:, :], psAV[:, 0:64], eng="scalar")
                p.v(lambda e, dg=dg, rec=rec: e.tensor_scalar_mul(dg[:, :], p.ident[:, :], rec[:, 0:1]))
                p.mm(psT[0:64, :], aun[:, 0:64], dg[:, :])
                p.copy(attnT[(h % 2) * 64:(h % 2) * 64 + 64,
                             (h // 2) * S + tq * 128:(h // 2) * S + (tq + 1) * 128],
                       psT[0:64, :], eng="scalar")
    arB.reset()

    wob = arB.alloc(8 * 512)
    boS = arB.alloc(D)
    xr = arB.alloc(512)
    ob = arB.alloc(512)
    p.dma(boS[:, :], bo_bc)
    for c in range(4):
        for lt in range(8):
            p.dma(wob[:, lt * 512:(lt + 1) * 512], wo[lt * 128:(lt + 1) * 128, c * 512:(c + 1) * 512])
        for ti in range(TT):
            for lt in range(8):
                p.mm(ps4[0][:, :], attnT[:, lt * S + ti * 128:lt * S + (ti + 1) * 128],
                     wob[:, lt * 512:(lt + 1) * 512], start=(lt == 0), stop=(lt == 7))
            p.dma(xr[:, :], x[ti * 128:(ti + 1) * 128, c * 512:(c + 1) * 512])
            p.v(lambda e, ob=ob, c=c: e.tensor_add(ob[:, :], ps4[0][:, :], boS[:, c * 512:(c + 1) * 512]))
            p.v(lambda e, ob=ob, xr=xr: e.tensor_add(ob[:, :], ob[:, :], xr[:, :]))
            p.dma(out[ti * 128:(ti + 1) * 128, c * 512:(c + 1) * 512], ob[:, :], queue="gpsimd")
    p.finalize()
    return nc



def build_s7a(nc):
    x = nc.dram_tensor("x", [S, D], F32, kind="ExternalInput").ap()
    wqm = nc.dram_tensor("wqm", [D, D], F32, kind="ExternalInput").ap()
    wkm = nc.dram_tensor("wkm", [D, D], F32, kind="ExternalInput").ap()
    wvm = nc.dram_tensor("wvm", [D, D], F32, kind="ExternalInput").ap()
    wa = nc.dram_tensor("wa", [D, 16], F32, kind="ExternalInput").ap()
    wb_i = nc.dram_tensor("wb_i", [D, 16], F32, kind="ExternalInput").ap()
    ba_bc = nc.dram_tensor("ba_bc", [128, 16], F32, kind="ExternalInput").ap()
    bb_bc = nc.dram_tensor("bb_bc", [128, 16], F32, kind="ExternalInput").ap()
    qT_o = nc.dram_tensor("qT_o", [D, S], F32, kind="ExternalOutput").ap()
    kT_o = nc.dram_tensor("kT_o", [D, S], F32, kind="ExternalOutput").ap()
    vT_o = nc.dram_tensor("vT_o", [D, S], F32, kind="ExternalOutput").ap()
    beta_o = nc.dram_tensor("beta_o", [S, 16], F32, kind="ExternalOutput").ap()
    g_o = nc.dram_tensor("g_o", [S, 16], F32, kind="ExternalOutput").ap()

    p, C = start_prog(nc)
    ar = Arena(p, "bigA", 36900)
    arB = Arena(p, "bigB", 7100)
    ps4, psT, psE = C["ps4"], C["psT"], C["psE"]

    xnT = ar.alloc(DT * S)
    tmp = mk_ln_tmp(p, arB)
    emit_ln_chunk(p, C, x, xnT, tmp, 0, TT, S)
    arB.reset()

    wcb = ar.alloc(DT * 128)
    fq = ar.alloc(S)
    t1 = arB.alloc(128)
    s2 = arB.alloc(128)
    rn = arB.alloc(8)
    for (w_h, dst, do_norm) in ((wqm, qT_o, True), (wkm, kT_o, True), (wvm, vT_o, False)):
        for ec in range(16):
            for di in range(DT):
                p.dma(wcb[:, di * 128:(di + 1) * 128],
                      w_h[di * 128:(di + 1) * 128, ec * 128:(ec + 1) * 128])
            for ti in range(TT):
                for di in range(DT):
                    p.mm(psT[:, :], xnT[:, di * S + ti * 128:di * S + (ti + 1) * 128],
                         wcb[:, di * 128:(di + 1) * 128], start=(di == 0), stop=(di == DT - 1))
                p.act(t1[:, :], psT[:, :], AF.Silu)
                if do_norm:
                    p.act(s2[:, :], t1[:, :], AF.Square)
                    p.v(lambda e, s2=s2, rn=rn: e.reduce_sum(rn[:, 0:1], s2[:, :], axis=AX.X))
                    p.act(rn[:, 2:3], rn[:, 0:1], AF.Sqrt)
                    p.v(lambda e, rn=rn: e.tensor_scalar_max(rn[:, 4:5], rn[:, 2:3], 1e-12))
                    p.v(lambda e, rn=rn: e.reciprocal(rn[:, 6:7], rn[:, 4:5]))
                    p.v(lambda e, t1=t1, rn=rn: e.tensor_scalar_mul(t1[:, :], t1[:, :], rn[:, 6:7]))
                p.trans(psT[:, :], t1[:, :])
                p.copy(fq[:, ti * 128:(ti + 1) * 128], psT[:, :], eng="scalar")
            p.dma(dst[ec * 128:(ec + 1) * 128, :], fq[:, :], queue="gpsimd")
    wab = arB.alloc(DT * 32)
    bab = arB.alloc(32)
    r16 = arB.alloc(128)
    for di in range(DT):
        p.dma(wab[:, di * 32:di * 32 + 16], wa[di * 128:(di + 1) * 128, :])
        p.dma(wab[:, di * 32 + 16:di * 32 + 32], wb_i[di * 128:(di + 1) * 128, :])
    p.dma(bab[:, 0:16], ba_bc)
    p.dma(bab[:, 16:32], bb_bc)
    for ti in range(TT):
        for di in range(DT):
            p.mm(psE[:, 0:16], xnT[:, di * S + ti * 128:di * S + (ti + 1) * 128],
                 wab[:, di * 32:di * 32 + 16], start=(di == 0), stop=(di == DT - 1))
            p.mm(ps4[0][:, 0:16], xnT[:, di * S + ti * 128:di * S + (ti + 1) * 128],
                 wab[:, di * 32 + 16:di * 32 + 32], start=(di == 0), stop=(di == DT - 1))
        za = r16[:, 0:16]
        p.v(lambda e, za=za: e.tensor_add(za, psE[:, 0:16], bab[:, 0:16]))
        zb = r16[:, 16:32]
        p.v(lambda e, zb=zb: e.tensor_add(zb, ps4[0][:, 0:16], bab[:, 16:32]))
        p.act(r16[:, 32:48], zb, AF.Sigmoid)
        p.dma(beta_o[ti * 128:(ti + 1) * 128, :], r16[:, 32:48], queue="gpsimd")
        p.v(lambda e, za=za: e.tensor_scalar_min(r16[:, 48:64], za, 0.0))
        p.v(lambda e, za=za: e.tensor_scalar_mul(r16[:, 64:80], za, -1.0))
        p.v(lambda e, za=za: e.tensor_max(r16[:, 64:80], r16[:, 64:80], za))
        p.act(r16[:, 80:96], r16[:, 64:80], AF.Exp, scale=-1.0)
        p.act(r16[:, 96:112], r16[:, 80:96], AF.Ln, bias=1.0)
        p.v(lambda e: e.tensor_sub(r16[:, 112:128], r16[:, 48:64], r16[:, 96:112]))
        p.dma(g_o[ti * 128:(ti + 1) * 128, :], r16[:, 112:128], queue="gpsimd")
    p.finalize()
    return nc



def build_s7b(nc):
    CH = 4
    SL = CH * 128
    x = nc.dram_tensor("x", [SL, D], F32, kind="ExternalInput").ap()
    qT_i = nc.dram_tensor("qT_i", [D, SL], F32, kind="ExternalInput").ap()
    kT_i = nc.dram_tensor("kT_i", [D, SL], F32, kind="ExternalInput").ap()
    vT_i = nc.dram_tensor("vT_i", [D, SL], F32, kind="ExternalInput").ap()
    beta_i = nc.dram_tensor("beta_i", [SL, 16], F32, kind="ExternalInput").ap()
    g_i = nc.dram_tensor("g_i", [SL, 16], F32, kind="ExternalInput").ap()
    lam_i = nc.dram_tensor("lam_i", [16, 128], F32, kind="ExternalInput").ap()
    onw = nc.dram_tensor("onw", [128, 1], F32, kind="ExternalInput").ap()
    wo = nc.dram_tensor("wo", [D, D], F32, kind="ExternalInput").ap()
    ckk_i = nc.dram_tensor("ckk_i", [128, 16 * 128], F32, kind="ExternalInput").ap()
    ckv_i = nc.dram_tensor("ckv_i", [128, 16 * 128], F32, kind="ExternalInput").ap()
    lt_c = nc.dram_tensor("lt_c", [128, 128], F32, kind="ExternalInput").ap()
    ident_h = nc.dram_tensor("c_ident", [128, 128], F32, kind="ExternalInput").ap()
    out = nc.dram_tensor("out", [SL, D], F32, kind="ExternalOutput").ap()
    ckk_o = nc.dram_tensor("ckk_o", [128, 16 * 128], F32, kind="ExternalOutput").ap()
    ckv_o = nc.dram_tensor("ckv_o", [128, 16 * 128], F32, kind="ExternalOutput").ap()
    DBG = bool(globals().get("S7B_DEBUG", False))
    if DBG:
        xst_o = nc.dram_tensor("xst_o", [128, CH * 2048], F32, kind="ExternalOutput").ap()
        ptb_o = nc.dram_tensor("ptb_o", [128, CH * 2048], F32, kind="ExternalOutput").ap()
        wm_o = nc.dram_tensor("wm_o", [128, CH * 2048], F32, kind="ExternalOutput").ap()
    b1_d = nc.dram_tensor("b1_d", [1, 6144], F32, kind="Internal").ap()
    b2_d = nc.dram_tensor("b2_d", [1, 2048], F32, kind="Internal").ap()
    b3_d = nc.dram_tensor("b3_d", [1, 4096], F32, kind="Internal").ap()
    b4_d = nc.dram_tensor("b4_d", [1, 2048], F32, kind="Internal").ap()

    p = P(nc)
    p.ident = p.sbuf("identS", [128, 128], F32)
    p.dma(p.ident[:, :], ident_h)
    psQ = p.psum("psQ", [128, 2048], F32)
    psS = p.psum("psS", [128, 128], F32)
    psX = p.psum("psX", [128, 128], F32)
    psRed = p.psum("psRed", [128, 16], F32)
    psG = p.psum("psG", [16, 128], F32)

    def Q(h):
        return psQ[:, h * 128:(h + 1) * 128]

    ar = Arena(p, "bigA", 36600)
    Kf = ar.alloc(2048); Vf = ar.alloc(2048); Qf = ar.alloc(2048)
    Ktok = ar.alloc(2048); Vtok = ar.alloc(2048); K2tok = ar.alloc(2048); Kw = ar.alloc(2048)
    Wm = ar.alloc(2048); Ptb = ar.alloc(2048)
    Xst = ar.alloc(2048); Rst = ar.alloc(2048); Pst = ar.alloc(2048)
    scrA = ar.alloc(2048); scrB = ar.alloc(2048)
    dl = ar.alloc(2048)
    Ckk = ar.alloc(2048); CkvT = ar.alloc(2048)
    ltS = ar.alloc(128)
    wob = ar.alloc(512)
    xr = ar.alloc(512)
    sdv = ar.alloc(128)
    rowm = ar.alloc(256)
    pk = p.sbuf("pk", [16, 512], F32)
    brow = p.sbuf("brow", [1, 6144], F32)
    brow2 = p.sbuf("brow2", [1, 4096], F32)
    onesrow = p.sbuf("onesrow", [1, 128], F32)
    onescol = p.sbuf("onescol", [128, 1], F32)
    lamS = p.sbuf("lamS", [16, 128], F32)
    lamT = p.sbuf("lamT", [128, 16], F32)
    onwS = p.sbuf("onwS", [128, 1], F32)
    bt = p.sbuf("bt", [128, 64], F32)
    gt = p.sbuf("gt", [128, 64], F32)
    eG = p.sbuf("eG", [16, 128], F32)
    enb = p.sbuf("enb", [16, 128], F32)
    dbc = p.sbuf("dbc", [128, 16], F32)
    wpc = p.sbuf("wpc", [128, 16], F32)
    ebT = p.sbuf("ebT", [128, 16], F32)

    p.v(lambda e: e.memset(onesrow[:, :], 1.0))
    p.v(lambda e: e.memset(onescol[:, :], 1.0))
    p.dma(ltS[:, :], lt_c)
    p.dma(onwS[:, :], onw)
    p.dma(Ckk[:, :], ckk_i)
    p.dma(CkvT[:, :], ckv_i)
    p.dma(bt[:, :].rearrange("p (n h) -> p n h", h=16), beta_i.rearrange("(n p) h -> p n h", p=128))
    p.dma(gt[:, :].rearrange("p (n h) -> p n h", h=16), g_i.rearrange("(n p) h -> p n h", p=128))
    p.dma(lamS[:, :], lam_i)
    t16 = pk
    p.v(lambda e: e.tensor_scalar_mul(t16[0:16, 0:128], lamS[:, :], -1.0))
    p.v(lambda e: e.tensor_max(t16[0:16, 0:128], t16[0:16, 0:128], lamS[:, :]))
    p.act(t16[0:16, 256:384], t16[0:16, 0:128], AF.Exp, scale=-1.0)
    p.act(t16[0:16, 384:512], t16[0:16, 256:384], AF.Ln, bias=1.0)
    p.v(lambda e: e.tensor_scalar_max(t16[0:16, 128:256], lamS[:, :], 0.0))
    p.v(lambda e: e.tensor_add(lamS[:, :], t16[0:16, 384:512], t16[0:16, 128:256]))
    p.v(lambda e: e.tensor_scalar_add(lamS[:, :], lamS[:, :], 0.25))
    p.trans(psX[:, 0:16], lamS[:, :])
    p.copy(lamT[:, :], psX[:, 0:16], eng="scalar")
    for h in range(16):
        p.v(lambda e, h=h: e.tensor_scalar_mul(dl[:, h * 128:(h + 1) * 128], p.ident[:, :], lamT[:, h:h + 1]))

    NITER = 30
    rows16 = p.sbuf("rows16", [16, 512], F32)

    def emit_mv(src):
        p.v(lambda e: e.tensor_mul(scrA[:, :], Ptb[:, :], src[:, :]))
        for h in range(16):
            sl = slice(h * 128, (h + 1) * 128)
            p.mm(psS[:, :], Kf[:, sl], src[:, sl])
            p.v(lambda e, sl=sl: e.tensor_mul(sdv[:, :], psS[:, :], Wm[:, sl]))
            p.mm(Q(h), Ckk[:, sl], scrA[:, sl], start=True, stop=False)
            p.mm(Q(h), dl[:, sl], src[:, sl], start=False, stop=False)
            p.mm(Q(h), Ktok[:, sl], sdv[:, :], start=False, stop=True)

    def red16(src, dst16):
        for h in range(16):
            p.mm(psRed[:, h:h + 1], src[:, h * 128:(h + 1) * 128], onescol[:, :],
                 start=True, stop=True)
        p.copy(dst16, psRed[:, :], eng="scalar")

    def bounce(cols, dram, dst_row):
        p.dma(dram[0:1, 0:16 * cols].rearrange("a (h c) -> (a h) c", h=16), pk[0:16, 0:cols],
              queue="gpsimd")
        p.dma(dst_row[0:1, 0:16 * cols], dram[0:1, 0:16 * cols])

    for n in range(CH):
        csl = slice(n * 128, (n + 1) * 128)
        p.mm(psG[:, :], gt[:, n * 16:(n + 1) * 16], ltS[:, :])
        gam_r = rows16[0:16, 0:128]
        p.copy(gam_r, psG[:, :], eng="scalar")
        p.act(eG[:, :], gam_r, AF.Exp)
        p.trans(psG[:, :], bt[:, n * 16:(n + 1) * 16])
        br_ = rows16[0:16, 128:256]
        p.copy(br_, psG[:, :], eng="scalar")
        p.trans(psX[0:1, 0:16], eG[:, 127:128])
        drow = rows16[0:1, 256:272]
        p.copy(drow, psX[0:1, 0:16], eng="scalar")
        p.op("tensor", lambda e: e.matmul(psRed[:, :], onesrow[:, :], drow, start=True, stop=True))
        p.copy(dbc[:, :], psRed[:, :], eng="scalar")
        wex = rows16[0:16, 272:400]
        p.v(lambda e, gam_r=gam_r: e.tensor_scalar_sub(wex, gam_r, gam_r[:, 127:128]))
        p.act(wex, wex, AF.Exp, scale=-1.0)
        p.v(lambda e: e.tensor_mul(enb[:, :], wex, br_))
        p.trans(psX[:, 0:16], enb[:, :])
        p.copy(wpc[:, :], psX[:, 0:16], eng="scalar")
        p.v(lambda e, gam_r=gam_r: e.tensor_scalar_mul(pk[0:16, 128:256], gam_r, -1.0))
        p.copy(pk[0:16, 0:128], gam_r, eng="scalar")
        p.copy(pk[0:16, 256:384], eG[:, :], eng="scalar")
        bounce(384, b1_d, brow)
        for h in range(16):
            sl = slice(h * 128, (h + 1) * 128)
            p.mm(psS[:, :], onesrow[:, :], brow[0:1, h * 384:h * 384 + 128],
                 start=True, stop=False)
            p.mm(psS[:, :], brow[0:1, h * 384 + 128:h * 384 + 256], onesrow[:, :],
                 start=False, stop=True)
            p.v(lambda e: e.tensor_scalar_min(sdv[:, :], psS[:, :], 0.0))
            p.act(sdv[:, :], sdv[:, :], AF.Exp)
            p.v(lambda e, sl=sl: e.tensor_mul(Wm[:, sl], sdv[:, :], ltS[:, :]))
            p.v(lambda e, sl=sl, h=h, n=n: e.tensor_scalar_mul(Wm[:, sl], Wm[:, sl],
                                                               bt[:, n * 16 + h:n * 16 + h + 1]))
            p.mm(psS[:, :], onesrow[:, :], brow[0:1, h * 384:h * 384 + 128])
            p.act(Ptb[:, sl], psS[:, :], AF.Exp)
        for h in range(16):
            sl = slice(h * 128, (h + 1) * 128)
            p.dma(Kf[:, sl], kT_i[h * 128:(h + 1) * 128, csl])
            p.dma(Vf[:, sl], vT_i[h * 128:(h + 1) * 128, csl])
            p.dma(Qf[:, sl], qT_i[h * 128:(h + 1) * 128, csl])
            p.trans(psX[:, :], Kf[:, sl])
            p.copy(Ktok[:, sl], psX[:, :], eng="scalar")
            p.trans(psX[:, :], Vf[:, sl])
            p.copy(Vtok[:, sl], psX[:, :], eng="scalar")
        p.act(K2tok[:, :], Ktok[:, :], AF.Square)
        for h in range(16):
            sl = slice(h * 128, (h + 1) * 128)
            p.v(lambda e, sl=sl, h=h: e.tensor_scalar_mul(Kw[:, sl], Ktok[:, sl], wpc[:, h:h + 1]))
        for h in range(16):
            sl = slice(h * 128, (h + 1) * 128)
            p.v(lambda e, sl=sl: e.tensor_mul(scrA[:, 0:128], Ckk[:, sl], p.ident[:, :]))
            p.v(lambda e: e.reduce_sum(rowm[:, 0:1], scrA[:, 0:128], axis=AX.X))
            p.trans(psX[0:1, 0:128], rowm[:, 0:1])
            p.copy(brow2[0:1, 0:128], psX[0:1, 0:128], eng="scalar")
            p.mm(Q(h), brow2[0:1, 0:128], brow[0:1, h * 384 + 256:h * 384 + 384], start=True, stop=False)
            p.mm(Q(h), K2tok[:, sl], Wm[:, sl], start=False, stop=True)
            p.v(lambda e, sl=sl, h=h: e.tensor_scalar(scrB[:, 0:128], Q(h), lamT[:, h:h + 1], 1e-8,
                                                      ALU.add, ALU.add))
            p.v(lambda e: e.reciprocal(scrB[:, 128:256], scrB[:, 0:128]))
            p.v(lambda e, sl=sl: e.tensor_mul(Xst[:, sl], Qf[:, sl], scrB[:, 128:256]))
        emit_mv(Xst)
        p.v(lambda e: e.tensor_sub(Rst[:, :], Qf[:, :], psQ[:, :]))
        p.copy(Pst[:, :], Rst[:, :], eng="scalar")
        p.act(scrA[:, :], Rst[:, :], AF.Square)
        dcur, dalt = rowm[:, 0:16], rowm[:, 224:240]
        red16(scrA, dcur)
        done = rowm[:, 16:32]
        p.v(lambda e, dcur=dcur, done=done: e.tensor_single_scalar(done, dcur, -1.0, ALU.is_lt))
        pq, dn, nd, al, bet, gam = (rowm[:, 32:48], rowm[:, 48:64], rowm[:, 64:80],
                                    rowm[:, 80:96], rowm[:, 96:112], rowm[:, 112:128])
        tA, tB, spq = rowm[:, 128:144], rowm[:, 144:160], rowm[:, 160:176]
        ms, rs = rowm[:, 176:192], rowm[:, 192:208]
        for it in range(NITER):
            emit_mv(Pst)
            p.copy(scrB[:, :], psQ[:, :], eng="scalar")
            p.v(lambda e: e.tensor_mul(scrA[:, :], Pst[:, :], scrB[:, :]))
            red16(scrA, pq)
            p.v(lambda e, dcur=dcur: e.tensor_single_scalar(tA, dcur, 1e-10, ALU.is_lt))
            p.v(lambda e: e.tensor_max(done, done, tA))
            p.v(lambda e: e.tensor_scalar_mul(tB, pq, -1.0))
            p.v(lambda e: e.tensor_max(tB, tB, pq))
            p.v(lambda e: e.tensor_single_scalar(tA, tB, 1e-10, ALU.is_lt))
            p.v(lambda e: e.tensor_max(done, done, tA))
            p.v(lambda e: e.tensor_scalar(nd, done, -1.0, 1.0, ALU.mult, ALU.add))
            p.v(lambda e: e.tensor_mul(spq, pq, nd))
            p.v(lambda e: e.tensor_add(spq, spq, done))
            p.v(lambda e: e.reciprocal(tB, spq))
            p.v(lambda e, dcur=dcur: e.tensor_mul(al, dcur, tB))
            p.v(lambda e: e.tensor_mul(al, al, nd))
            p.trans(psG[:, :], al)
            p.copy(pk[0:16, 0:128], psG[:, :], eng="scalar")
            bounce(128, b2_d, brow2)
            for h in range(16):
                p.mm(Q(h), onesrow[:, :], brow2[0:1, h * 128:(h + 1) * 128])
            p.v(lambda e: e.tensor_mul(scrA[:, :], psQ[:, :], Pst[:, :]))
            p.v(lambda e: e.tensor_add(Xst[:, :], Xst[:, :], scrA[:, :]))
            p.v(lambda e: e.tensor_mul(scrA[:, :], psQ[:, :], scrB[:, :]))
            p.v(lambda e: e.tensor_sub(Rst[:, :], Rst[:, :], scrA[:, :]))
            p.act(scrA[:, :], Rst[:, :], AF.Square)
            red16(scrA, dn)
            p.v(lambda e, dcur=dcur: e.tensor_mul(spq, dcur, nd))
            p.v(lambda e: e.tensor_add(spq, spq, done))
            p.v(lambda e: e.reciprocal(tB, spq))
            p.v(lambda e: e.tensor_mul(bet, dn, tB))
            p.v(lambda e: e.tensor_mul(bet, bet, nd))
            p.v(lambda e: e.tensor_add(gam, bet, done))
            p.v(lambda e, dcur=dcur: e.tensor_sub(tA, dcur, dn))
            p.v(lambda e: e.tensor_mul(tA, tA, done))
            p.v(lambda e, dalt=dalt: e.tensor_add(dalt, dn, tA))
            dcur, dalt = dalt, dcur
            p.trans(psG[:, :], gam)
            p.copy(pk[0:16, 0:128], psG[:, :], eng="scalar")
            p.trans(psG[:, :], nd)
            p.copy(pk[0:16, 128:256], psG[:, :], eng="scalar")
            bounce(256, b3_d, brow)
            for h in range(16):
                sl = slice(h * 128, (h + 1) * 128)
                p.mm(psS[:, :], onesrow[:, :], brow[0:1, h * 256 + 128:h * 256 + 256])
                p.v(lambda e, sl=sl: e.tensor_mul(scrA[:, sl], psS[:, :], Rst[:, sl]))
                p.mm(Q(h), onesrow[:, :], brow[0:1, h * 256:h * 256 + 128])
            p.v(lambda e: e.tensor_mul(scrB[:, :], psQ[:, :], Pst[:, :]))
            p.v(lambda e: e.tensor_add(Pst[:, :], scrA[:, :], scrB[:, :]))
        if DBG:
            p.dma(xst_o[:, n * 2048:(n + 1) * 2048], Xst[:, :], queue="gpsimd")
            p.dma(ptb_o[:, n * 2048:(n + 1) * 2048], Ptb[:, :], queue="gpsimd")
            p.dma(wm_o[:, n * 2048:(n + 1) * 2048], Wm[:, :], queue="gpsimd")
        p.v(lambda e: e.tensor_mul(scrA[:, :], Ptb[:, :], Xst[:, :]))
        for h in range(16):
            sl = slice(h * 128, (h + 1) * 128)
            p.mm(psS[:, :], Vf[:, sl], Xst[:, sl])
            p.v(lambda e, sl=sl: e.tensor_mul(sdv[:, :], psS[:, :], Wm[:, sl]))
            p.mm(Q(h), Ktok[:, sl], sdv[:, :], start=True, stop=False)
            p.mm(Q(h), CkvT[:, sl], scrA[:, sl], start=False, stop=True)
        p.act(scrB[:, :], psQ[:, :], AF.Square)
        red16(scrB, ms)
        p.act(tA, ms, AF.Sqrt, bias=1e-6, scale=1.0 / 128.0)
        p.v(lambda e: e.reciprocal(rs, tA))
        p.copy(scrA[:, :], psQ[:, :], eng="scalar")
        p.trans(psG[:, :], rs)
        p.copy(pk[0:16, 0:128], psG[:, :], eng="scalar")
        bounce(128, b4_d, brow2)
        for h in range(16):
            p.mm(Q(h), onesrow[:, :], brow2[0:1, h * 128:(h + 1) * 128])
        p.v(lambda e: e.scalar_tensor_tensor(scrB[:, :], psQ[:, :], onwS[:, 0:1], scrA[:, :],
                                             ALU.mult, ALU.mult))
        for h in range(16):
            for dc in range(4):
                p.dma(wob[:, :], wo[h * 128:(h + 1) * 128, dc * 512:(dc + 1) * 512])
                p.mm(psQ[:, dc * 512:(dc + 1) * 512], scrB[:, h * 128:(h + 1) * 128], wob[:, :],
                     start=(h == 0), stop=(h == 15))
        for dc in range(4):
            p.dma(xr[:, :], x[n * 128:(n + 1) * 128, dc * 512:(dc + 1) * 512])
            p.v(lambda e, dc=dc: e.tensor_add(scrA[:, 0:512], psQ[:, dc * 512:(dc + 1) * 512], xr[:, :]))
            p.dma(out[n * 128:(n + 1) * 128, dc * 512:(dc + 1) * 512], scrA[:, 0:512], queue="gpsimd")
        for h in range(16):
            sl = slice(h * 128, (h + 1) * 128)
            p.mm(psS[:, :], Kw[:, sl], Ktok[:, sl])
            p.v(lambda e, sl=sl, h=h: e.scalar_tensor_tensor(Ckk[:, sl], Ckk[:, sl], dbc[:, h:h + 1],
                                                             psS[:, :], ALU.mult, ALU.add))
            p.mm(psS[:, :], Vtok[:, sl], Kw[:, sl])
            p.v(lambda e, sl=sl, h=h: e.scalar_tensor_tensor(CkvT[:, sl], CkvT[:, sl], dbc[:, h:h + 1],
                                                             psS[:, :], ALU.mult, ALU.add))
    p.dma(ckk_o, Ckk[:, :], queue="gpsimd")
    p.dma(ckv_o, CkvT[:, :], queue="gpsimd")
    p.finalize()
    return nc



def build_masks():
    m = np.zeros((128, 4, 512), np.float32)
    f = np.arange(512)
    for r in range(4):
        for pp in range(128):
            m[pp, r] = (f >= r * 128 + pp).astype(np.float32)
    return np.ascontiguousarray(m.reshape(128, 2048))


def build_rope_tables():
    half, d = 32, 64
    inv = 10000.0 ** (-np.arange(half, dtype=np.float32) * 2.0 / d)
    ang = np.arange(S, dtype=np.float32)[:, None] * inv[None, :]
    cosT = np.cos(ang).T.astype(np.float32)
    sinT = np.sin(ang).T.astype(np.float32)
    csA = np.concatenate([cosT, sinT], 0)
    csB = np.concatenate([-sinT, cosT], 0)
    return np.ascontiguousarray(np.concatenate([csA, csB], 1))


def rope_perm():
    return np.concatenate([np.arange(0, 64, 2), np.arange(1, 64, 2)])


def rope_perm_dup():
    ev, od = np.arange(0, 64, 2), np.arange(1, 64, 2)
    return np.concatenate([ev, ev]), np.concatenate([od, od])


def build_lt():
    s = np.arange(128)
    return np.ascontiguousarray((s[:, None] <= s[None, :]).astype(np.float32))


def _T(a):
    return np.ascontiguousarray(a.T)


def _rep(v):
    return np.ascontiguousarray(np.broadcast_to(v[None, :], (128, v.shape[0])).astype(np.float32))


def _col(v):
    return np.ascontiguousarray(v[:, None].astype(np.float32))


_BUILT = {}


def _prog(name, builder):
    if name not in _BUILT:
        nc = bass.Bass("TRN2", target_bir_lowering=False)
        builder(nc)
        _BUILT[name] = nc
    return _BUILT[name]


def _run(name, builder, per_core, shared):
    nc = _prog(name, builder)
    in_maps = [dict(shared, **per_core[i]) for i in range(NCORE)]
    res = run_bass_kernel_spmd(nc, in_maps, core_ids=list(range(NCORE)))
    return res.results


def kernel(**inputs):
    gi = {k: np.ascontiguousarray(np.asarray(v, dtype=np.float32)) for k, v in inputs.items()}
    ident = np.eye(128, dtype=np.float32)
    masks = build_masks()
    cs_c = build_rope_tables()
    lt = build_lt()

    xs = [np.ascontiguousarray(gi["L0_attn_a_x"][i]) for i in range(NCORE)]

    pA, pB = rope_perm_dup()
    pAB = np.concatenate([pA, pB])
    wrqTp = _T(gi["L0_attn_a_W_rq"])
    wrqTp = np.ascontiguousarray(wrqTp.reshape(512, 16, 64)[:, :, pAB].reshape(512, 2048))
    wrkTp = np.ascontiguousarray(_T(gi["L0_attn_a_W_rk"])[:, pAB])
    shared = {"wdqT": _T(gi["L0_attn_a_W_dq"]), "wdkvT": _T(gi["L0_attn_a_W_dkv"]),
              "wuqT": _T(gi["L0_attn_a_W_uq"]), "wukT": _T(gi["L0_attn_a_W_uk"]),
              "wuvT": _T(gi["L0_attn_a_W_uv"]), "wrqTp": wrqTp, "wrkTp": wrkTp,
              "woT": _T(gi["L0_attn_a_W_o"]), "cs_c": cs_c,
              "c_ident": ident, "c_masks": masks}
    res = _run("s1", build_s1, [{"x": xs[i]} for i in range(NCORE)], shared)
    xs = [res[i]["out"] for i in range(NCORE)]

    ffn_shared = {"gwT": _T(gi["L0_ffn_gate_w"]), "uwT": _T(gi["L0_ffn_up_w"]),
                  "dwT": _T(gi["L0_ffn_down_w"]), "gs": _col(gi["L0_ffn_gate_s"]),
                  "us": _col(gi["L0_ffn_up_s"]), "ds_bc": _rep(gi["L0_ffn_down_s"]),
                  "c_ident": ident}
    res = _run("s2", build_s2, [{"x": xs[i]} for i in range(NCORE)], ffn_shared)
    xs = [res[i]["out"] for i in range(NCORE)]

    shared = {"wk": gi["L0_attn_b_wk"], "wq": gi["L0_attn_b_wq"], "wv": gi["L0_attn_b_wv"],
              "bk": _col(gi["L0_attn_b_bk"]), "bq": _col(gi["L0_attn_b_bq"]),
              "bv": _col(gi["L0_attn_b_bv"]), "wkl": gi["L0_attn_b_wkl"],
              "wql": gi["L0_attn_b_wql"], "wvl": gi["L0_attn_b_wvl"],
              "bkl": _col(gi["L0_attn_b_bkl"]), "bql": _col(gi["L0_attn_b_bql"]),
              "bvl_bc": _rep(gi["L0_attn_b_bvl"]), "wo": gi["L0_attn_b_wo"],
              "bo_bc": _rep(gi["L0_attn_b_bo"]), "c_ident": ident, "c_masks": masks}
    res = _run("s3", build_s3, [{"x": xs[i]} for i in range(NCORE)], shared)
    xs = [res[i]["out"] for i in range(NCORE)]

    res = _run("s2", build_s2, [{"x": xs[i]} for i in range(NCORE)], ffn_shared)
    xs = [res[i]["out"] for i in range(NCORE)]

    shared = {"wqkvT": _T(gi["L1_attn_a_Wqkv"]), "wprojT": _T(gi["L1_attn_a_Wproj"]),
              "bproj_bc": _rep(gi["L1_attn_a_bproj"]), "c_ident": ident}
    res = _run("s5", build_s5, [{"x": xs[i]} for i in range(NCORE)], shared)
    xs = [res[i]["out"] for i in range(NCORE)]

    w1T = np.ascontiguousarray(np.concatenate([_T(gi["L1_ffn_W1"][e]) for e in range(8)], 0))
    w2T = np.ascontiguousarray(np.concatenate([_T(gi["L1_ffn_W2"][e]) for e in range(8)], 0))
    w3T = np.ascontiguousarray(np.concatenate([_T(gi["L1_ffn_W3"][e]) for e in range(8)], 0))
    moe_shared = {"wgT": _T(gi["L1_ffn_Wg"]), "w1T": w1T, "w2T": w2T, "w3T": w3T,
                  "c_ident": ident}
    res = _run("s6", build_s6, [{"x": xs[i]} for i in range(NCORE)], moe_shared)
    xs = [res[i]["out"] for i in range(NCORE)]

    shared = {"wqm": gi["L1_attn_b_Wq"], "wkm": gi["L1_attn_b_Wk"], "wvm": gi["L1_attn_b_Wv"],
              "wa": gi["L1_attn_b_Wa"], "wb_i": gi["L1_attn_b_Wb"],
              "ba_bc": _rep(gi["L1_attn_b_ba"]), "bb_bc": _rep(gi["L1_attn_b_bb"]),
              "c_ident": ident}
    res = _run("s7a", build_s7a, [{"x": xs[i]} for i in range(NCORE)], shared)
    proj = res
    lam_i = np.ascontiguousarray(gi["L1_attn_b_lambda_params"].reshape(16, 128))
    onw = _col(gi["L1_attn_b_o_norm_weight"])
    ckk = [np.zeros((128, 2048), np.float32) for _ in range(NCORE)]
    ckv = [np.zeros((128, 2048), np.float32) for _ in range(NCORE)]
    outs7 = [np.zeros((S, D), np.float32) for _ in range(NCORE)]
    shared_b = {"lam_i": lam_i, "onw": onw, "wo": gi["L1_attn_b_Wo"], "lt_c": lt,
                "c_ident": ident}
    for lch in range(4):
        sl = slice(lch * 512, (lch + 1) * 512)
        per_core = []
        for i in range(NCORE):
            per_core.append({
                "x": np.ascontiguousarray(xs[i][sl, :]),
                "qT_i": np.ascontiguousarray(proj[i]["qT_o"][:, sl]),
                "kT_i": np.ascontiguousarray(proj[i]["kT_o"][:, sl]),
                "vT_i": np.ascontiguousarray(proj[i]["vT_o"][:, sl]),
                "beta_i": np.ascontiguousarray(proj[i]["beta_o"][sl, :]),
                "g_i": np.ascontiguousarray(proj[i]["g_o"][sl, :]),
                "ckk_i": ckk[i], "ckv_i": ckv[i],
            })
        res = _run("s7b", build_s7b, per_core, shared_b)
        for i in range(NCORE):
            outs7[i][sl, :] = res[i]["out"]
            ckk[i] = res[i]["ckk_o"]
            ckv[i] = res[i]["ckv_o"]
    xs = outs7

    res = _run("s6", build_s6, [{"x": xs[i]} for i in range(NCORE)], moe_shared)
    xs = [res[i]["out"] for i in range(NCORE)]

    return np.stack(xs, axis=0).astype(np.float32)
